# version 01
# baseline (speedup 1.0000x reference)
import contextlib
import math
import numpy as np
import ml_dtypes
import concourse.bass as bass
import concourse.mybir as mybir
from concourse.bass_utils import run_bass_kernel_spmd

F32 = mybir.dt.float32
BF16 = mybir.dt.bfloat16
AF = mybir.ActivationFunctionType
ALU = mybir.AluOpType
BF = ml_dtypes.bfloat16

S = 4096
H = 4096
D = 128
I = 11008
ISUB = I // 128
HT = H // 128
TILES = S // 128
NPASS = 4
EPS = 1e-5
SCALE = 1.0 / math.sqrt(D)
NEG = -1e9

ENGINES = ["sync", "tensor", "vector", "scalar", "gpsimd"]
SEMS = {"pe": 1, "dv": 1, "ac": 1, "po": 1, "qs": 16, "qw": 16, "st": 16}
ENG_SEM = {"tensor": "pe", "vector": "dv", "scalar": "ac", "gpsimd": "po"}


class Buf:
    __slots__ = ("name", "writer", "readers")

    def __init__(self, name):
        self.name = name
        self.writer = None
        self.readers = []


class Rec:
    def __init__(self):
        self.streams = {e: [] for e in ENGINES}
        self.count = {s: 0 for s in SEMS}
        self.waited = {e: {s: 0 for s in SEMS} for e in ENGINES}

    def _w(self, eng, waits, tok):
        if tok is None:
            return
        sem, val = tok
        if eng == "tensor" and sem == "pe":
            return
        if self.waited[eng][sem] >= val:
            return
        self.waited[eng][sem] = val
        waits.append((sem, val))

    def emit(self, eng, method, args=(), kwargs=None, reads=(), writes=(),
             sem=None):
        waits = []
        for b in reads:
            self._w(eng, waits, b.writer)
        for b in writes:
            self._w(eng, waits, b.writer)
            for t in b.readers:
                self._w(eng, waits, t)
        if sem is None:
            sem = ENG_SEM.get(eng, "qs")
        inc = SEMS[sem]
        self.count[sem] += inc
        tok = (sem, self.count[sem])
        self.streams[eng].append((waits, method, args, kwargs or {}, sem, inc))
        for b in reads:
            b.readers.append(tok)
        for b in writes:
            b.writer = tok
            b.readers = []
        return tok

    def raw_wait(self, eng, sem, val):
        if val <= 0 or self.waited[eng][sem] >= val:
            return
        self.waited[eng][sem] = val
        self.streams[eng].append(([(sem, val)], None, (), {}, None, 0))

    def barrier(self):
        snap = dict(self.count)
        for e in ENGINES:
            for s, v in snap.items():
                self.raw_wait(e, s, v)


def build_nc():
    nc = bass.Bass("TRN2", target_bir_lowering=False, debug=False)
    R = Rec()

    di = lambda n, sh, dt: nc.dram_tensor(n, sh, dt, kind="ExternalInput").ap()
    xp = di("xp", [S, H], F32)
    xq = di("xq", [1024, H], F32)
    wq4 = di("wq4", [32, 128, 4096], F32)
    wk4 = di("wk4", [NPASS, 128, HT * 256], F32)
    wv4 = di("wv4", [NPASS, 128, HT * 256], F32)
    wo4 = di("wo4", [8, 32, 128, 512], F32)
    gate4 = di("gate4", [ISUB, 128, 4096], BF16)
    up4 = di("up4", [ISUB, 128, 4096], BF16)
    down4 = di("down4", [32, 128, I], BF16)
    gs_d = di("gs", [128, ISUB], F32)
    us_d = di("us", [128, ISUB], F32)
    ds_d = di("ds", [128, 32], F32)
    cosq_d = di("cosq", [128, 1024], F32)
    sinq_d = di("sinq", [128, 1024], F32)
    ck_d = di("ck", [S, 128], F32)
    sk_d = di("sk", [S, 128], F32)
    dmat_d = di("dmat", [128, 512], F32)
    theta_d = di("theta", [128, 32], F32)
    rt_d = di("rt", [128, 128], F32)
    idb_d = di("idb", [128, 128], BF16)
    out_d = nc.dram_tensor("out", [1024, H], F32, kind="ExternalOutput").ap()
    attn_dram = nc.dram_tensor("attscr", [32, 8, 128, 128], BF16,
                               kind="Internal").ap()

    B = {}

    def buf(n):
        if n not in B:
            B[n] = Buf(n)
        return B[n]

    def dma(eng, dst_ap, src_ap, reads, writes, sem):
        return R.emit(eng, "dma_start", (), {"out": dst_ap, "in_": src_ap},
                      reads=reads, writes=writes, sem=sem)

    es = contextlib.ExitStack()
    with es:
        sbt = lambda st_, n, sh, dt: st_.enter_context(nc.sbuf_tensor(n, sh, dt))
        T0 = es.enter_context(nc.psum_tensor("T0", [128, 1024], F32))
        T1 = es.enter_context(nc.psum_tensor("T1", [128, 1024], F32))
        T2 = es.enter_context(nc.psum_tensor("T2", [128, 1024], F32))
        Tb = es.enter_context(nc.psum_tensor("Tb", [128, 1024], BF16))
        Tb2 = es.enter_context(nc.psum_tensor("Tb2", [128, 1024], BF16))
        semh = {s: es.enter_context(nc.semaphore(s)) for s in SEMS}
        bT = {"0": (buf("T0a"), buf("T0b")), "1": (buf("T1a"), buf("T1b")),
              "2": (buf("T2a"), buf("T2b"))}
        bTb, bTb2 = buf("Tb"), buf("Tb2")
        OUT = [[buf(f"out_{si}_{oc}") for oc in range(8)] for si in range(8)]
        battn = [[buf(f"at_{qh}_{si}") for si in range(8)] for qh in range(32)]

        ident_s = sbt(es, "ident_s", [128, 128], BF16)
        rt_s = sbt(es, "rt_s", [128, 128], F32)
        dmat_s = sbt(es, "dmat_s", [128, 512], F32)
        theta_s = sbt(es, "theta_s", [128, 32], F32)
        gs_s = sbt(es, "gs_s", [128, ISUB], F32)
        us_s = sbt(es, "us_s", [128, ISUB], F32)
        ds_s = sbt(es, "ds_s", [128, 32], F32)
        mr_s = sbt(es, "mr_s", [128, 80], F32)
        ln1 = sbt(es, "ln1", [128, 8], F32)
        for dst, src in [(ident_s, idb_d), (rt_s, rt_d), (dmat_s, dmat_d),
                         (theta_s, theta_d), (gs_s, gs_d), (us_s, us_d),
                         (ds_s, ds_d)]:
            dma("sync", dst[:], src, (), (buf(dst.name),), "qs")

        stats_done = set()

        def xn_build(slot, rows_ap, dst_view, xl, xnl, sq=None):
            bx, bxn = buf(xl.name), buf(xnl.name)
            bmr, bln = buf("mr_s"), buf("ln1")
            dma("sync", xl[:], rows_ap, (), (bx,), "qs")
            rc = mr_s[:, 2 * slot:2 * slot + 1]
            ng = mr_s[:, 2 * slot + 1:2 * slot + 2]
            if slot not in stats_done:
                stats_done.add(slot)
                bsq = buf(sq.name)
                R.emit("scalar", "activation", (sq[:], xl[:], AF.Square),
                       {"accum_out": ln1[:, 0:1]}, reads=(bx,),
                       writes=(bsq, bln))
                R.emit("vector", "tensor_reduce",
                       (ln1[:, 1:2], xl[:], mybir.AxisListType.X, ALU.add),
                       {}, reads=(bx,), writes=(bln,))
                R.emit("vector", "tensor_scalar",
                       (ln1[:, 2:3], ln1[:, 1:2], 1.0 / H, None, ALU.mult),
                       {}, reads=(bln,), writes=(bln,))
                R.emit("vector", "tensor_scalar",
                       (ln1[:, 3:4], ln1[:, 0:1], 1.0 / H, None, ALU.mult),
                       {}, reads=(bln,), writes=(bln,))
                R.emit("vector", "tensor_tensor",
                       (ln1[:, 4:5], ln1[:, 2:3], ln1[:, 2:3], ALU.mult),
                       {}, reads=(bln,), writes=(bln,))
                R.emit("vector", "tensor_tensor",
                       (ln1[:, 5:6], ln1[:, 3:4], ln1[:, 4:5], ALU.subtract),
                       {}, reads=(bln,), writes=(bln,))
                R.emit("vector", "tensor_scalar",
                       (ln1[:, 6:7], ln1[:, 5:6], 1.0, EPS, ALU.mult, ALU.add),
                       {}, reads=(bln,), writes=(bln,))
                R.emit("scalar", "activation",
                       (ln1[:, 7:8], ln1[:, 6:7], AF.Sqrt), {},
                       reads=(bln,), writes=(bln,))
                R.emit("vector", "reciprocal", (rc, ln1[:, 7:8]), {},
                       reads=(bln,), writes=(bmr,))
                R.emit("vector", "tensor_scalar",
                       (ln1[:, 0:1], ln1[:, 2:3], -1.0, None, ALU.mult),
                       {}, reads=(bln,), writes=(bln,))
                R.emit("vector", "tensor_tensor",
                       (ng, ln1[:, 0:1], rc, ALU.mult), {},
                       reads=(bln, bmr), writes=(bmr,))
            else:
                assert slot in stats_done
            R.emit("vector", "tensor_scalar",
                   (xnl[:], xl[:], rc, ng, ALU.mult, ALU.add), {},
                   reads=(bx, bmr), writes=(bxn,))
            bdst = buf(dst_view.tensor.name)
            for grp in range(4):
                pb, bb = (Tb, bTb) if grp % 2 == 0 else (Tb2, bTb2)
                for k in range(8):
                    ht = grp * 8 + k
                    R.emit("tensor", "transpose",
                           (pb[:, k * 128:(k + 1) * 128],
                            xnl[:, ht * 128:(ht + 1) * 128], ident_s[:]),
                           {}, reads=(bxn, buf("ident_s")), writes=(bb,))
                R.emit("vector", "tensor_copy",
                       (dst_view[:, grp * 8:grp * 8 + 8, :],
                        pb[:].rearrange("p (k t) -> p k t", k=8)),
                       {}, reads=(bb,), writes=(bdst,))

        for p in range(NPASS):
            esp = contextlib.ExitStack()
            with esp:
                QT = sbt(esp, f"QT{p}", [128, 8192], BF16)
                bQT = buf(QT.name)
                QTw = QT[:].rearrange("p (k s q t) -> p k s q t", k=2, s=8, q=4)
                QTr = QT[:].rearrange("p (k s t) -> p k s t", k=2, s=8)
                esb = contextlib.ExitStack()
                with esb:
                    cosq_s = sbt(esb, f"cosq{p}", [128, 1024], F32)
                    sinq_s = sbt(esb, f"sinq{p}", [128, 1024], F32)
                    dma("sync", cosq_s[:], cosq_d, (), (buf(cosq_s.name),), "qs")
                    dma("sync", sinq_s[:], sinq_d, (), (buf(sinq_s.name),), "qs")
                    x_l = [sbt(esb, f"bxl{p}{i}", [128, H], F32) for i in range(2)]
                    xn_l = [sbt(esb, f"bxn{p}{i}", [128, H], BF16) for i in range(2)]
                    sq_s = sbt(esb, f"bsq{p}", [128, H], BF16)
                    xnq = sbt(esb, f"xnq{p}", [128, HT * 512], BF16)
                    wql = [sbt(esb, f"wql{p}{i}", [128, 4096], F32) for i in range(2)]
                    wqb = [sbt(esb, f"wqb{p}{i}", [128, 4096], BF16) for i in range(2)]
                    qpre = sbt(esb, f"qpre{p}", [128, 512], F32)
                    qm1 = sbt(esb, f"qm1{p}", [128, 512], F32)
                    qm2 = sbt(esb, f"qm2{p}", [128, 512], F32)
                    bqpre, bqm1, bqm2 = buf(qpre.name), buf(qm1.name), buf(qm2.name)
                    xnqv = xnq[:].rearrange("p (h t) -> p h t", h=HT)
                    for qhalf in range(2):
                        for s4 in range(4):
                            si = qhalf * 4 + s4
                            xn_build(32 + si, xq[si * 128:(si + 1) * 128, :],
                                     xnqv[:, :, s4 * 128:(s4 + 1) * 128],
                                     x_l[s4 % 2], xn_l[s4 % 2], sq_s)
                        for ot in range(8):
                            qh = 8 * p + ot
                            wl, wb = wql[ot % 2], wqb[ot % 2]
                            bwl, bwb = buf(wl.name), buf(wb.name)
                            dma("gpsimd", wl[:], wq4[qh], (), (bwl,), "qw")
                            R.emit("gpsimd", "tensor_copy", (wb[:], wl[:]), {},
                                   reads=(bwl,), writes=(bwb,))
                            psq = T0[:, 0:512] if ot % 2 == 0 else T0[:, 512:1024]
                            bpsq = bT["0"][ot % 2]
                            for ht in range(HT):
                                lhs = wb[:, ht * 128:(ht + 1) * 128]
                                R.emit("tensor", "ldweights", (lhs,), {},
                                       reads=(bwb,))
                                R.emit("tensor", "matmul",
                                       (psq, lhs, xnqv[:, ht, :]),
                                       {"start": ht == 0, "stop": ht == HT - 1},
                                       reads=(bwb, buf(xnq.name)), writes=(bpsq,))
                            R.emit("vector", "tensor_copy", (qpre[:], psq), {},
                                   reads=(bpsq,), writes=(bqpre,))
                            rot = T1[:, 0:512] if ot % 2 == 0 else T1[:, 512:1024]
                            brot = bT["1"][ot % 2]
                            R.emit("tensor", "matmul", (rot, rt_s[:], qpre[:]),
                                   {"start": True, "stop": True},
                                   reads=(buf("rt_s"), bqpre), writes=(brot,))
                            cq = cosq_s[:, qhalf * 512:(qhalf + 1) * 512]
                            sq2 = sinq_s[:, qhalf * 512:(qhalf + 1) * 512]
                            R.emit("vector", "tensor_tensor",
                                   (qm1[:], qpre[:], cq, ALU.mult), {},
                                   reads=(bqpre, buf(cosq_s.name)), writes=(bqm1,))
                            R.emit("vector", "tensor_tensor",
                                   (qm2[:], rot, sq2, ALU.mult), {},
                                   reads=(brot, buf(sinq_s.name)), writes=(bqm2,))
                            kvl, q4 = ot // 4, ot % 4
                            R.emit("vector", "tensor_tensor",
                                   (QTw[:, kvl, qhalf * 4:qhalf * 4 + 4, q4, :],
                                    qm1[:].rearrange("p (s t) -> p s t", s=4),
                                    qm2[:].rearrange("p (s t) -> p s t", s=4),
                                    ALU.add),
                                   {}, reads=(bqm1, bqm2), writes=(bQT,))
                R.barrier()
                esc = contextlib.ExitStack()
                with esc:
                    KT = sbt(esc, f"KT{p}", [128, 8192], BF16)
                    Vt = sbt(esc, f"V{p}", [128, 32 * 260], BF16)
                    bKT, bV = buf(KT.name), buf(Vt.name)
                    KTv = KT[:].rearrange("p (k t) -> p k t", k=2)
                    Vv = Vt[:].rearrange("p (t k j) -> p t k j", k=2, j=130)
                    esc2 = contextlib.ExitStack()
                    with esc2:
                        x_l = [sbt(esc2, f"cxl{p}{i}", [128, H], F32)
                               for i in range(2)]
                        xn_l = [sbt(esc2, f"cxn{p}{i}", [128, H], BF16)
                                for i in range(2)]
                        sq_s = sbt(esc2, f"csq{p}", [128, H], BF16)
                        wkl = [sbt(esc2, f"wkl{p}{i}", [128, 2048], F32)
                               for i in range(2)]
                        wkb = sbt(esc2, f"wkb{p}", [128, HT * 256], BF16)
                        wvb = sbt(esc2, f"wvb{p}", [128, HT * 256], BF16)
                        xnt = [sbt(esc2, f"xnt{p}{i}", [128, HT * 128], BF16)
                               for i in range(2)]
                        ktmp = [sbt(esc2, f"ktmp{p}{i}", [128, 256], BF16)
                                for i in range(2)]
                        kw1 = sbt(esc2, f"kw1{p}", [128, 256], F32)
                        kw2 = sbt(esc2, f"kw2{p}", [128, 256], F32)
                        ck_l = [sbt(esc2, f"ckl{p}{i}", [128, 128], F32)
                                for i in range(2)]
                        sk_l = [sbt(esc2, f"skl{p}{i}", [128, 128], F32)
                                for i in range(2)]
                        bwkb, bwvb = buf(wkb.name), buf(wvb.name)
                        for wd, wbf, bb_ in [(wk4, wkb, bwkb), (wv4, wvb, bwvb)]:
                            for ch in range(4):
                                wl = wkl[ch % 2]
                                bwl = buf(wl.name)
                                dma("gpsimd", wl[:],
                                    wd[p][:, ch * 2048:(ch + 1) * 2048],
                                    (), (bwl,), "qw")
                                R.emit("gpsimd", "tensor_copy",
                                       (wbf[:, ch * 2048:(ch + 1) * 2048], wl[:]),
                                       {}, reads=(bwl,), writes=(bb_,))
                        R.emit("vector", "memset", (Vt[:], 1.0), {}, reads=(),
                               writes=(bV,))
                        for ti in range(TILES):
                            pp = ti % 2
                            xnow = xnt[pp]
                            bxn = buf(xnow.name)
                            xn_build(ti, xp[ti * 128:(ti + 1) * 128, :],
                                     xnow[:].rearrange("p (h t) -> p h t", h=HT),
                                     x_l[pp], xn_l[pp], sq_s)
                            PT_ = T0 if pp == 0 else T1
                            bk_, bv_ = bT[str(pp)]
                            for ht in range(HT):
                                lhs = xnow[:, ht * 128:(ht + 1) * 128]
                                R.emit("tensor", "ldweights", (lhs,), {},
                                       reads=(bxn,))
                                R.emit("tensor", "matmul",
                                       (PT_[:, 0:256], lhs,
                                        wkb[:, ht * 256:(ht + 1) * 256]),
                                       {"start": ht == 0, "stop": ht == HT - 1},
                                       reads=(bxn, bwkb), writes=(bk_,))
                                R.emit("tensor", "matmul",
                                       (PT_[:, 512:768], lhs,
                                        wvb[:, ht * 256:(ht + 1) * 256]),
                                       {"start": ht == 0, "stop": ht == HT - 1},
                                       reads=(bxn, bwvb), writes=(bv_,))
                            ckl, skl = ck_l[pp], sk_l[pp]
                            bck, bsk = buf(ckl.name), buf(skl.name)
                            dma("sync", ckl[:], ck_d[ti * 128:(ti + 1) * 128, :],
                                (), (bck,), "qs")
                            dma("sync", skl[:], sk_d[ti * 128:(ti + 1) * 128, :],
                                (), (bsk,), "qs")
                            kt = ktmp[pp]
                            bkt = buf(kt.name)
                            b1, b2 = buf(kw1.name), buf(kw2.name)
                            for kvl in range(2):
                                c0 = kvl * 128
                                R.emit("vector", "tensor_tensor",
                                       (kw1[:, c0:c0 + 64], PT_[:, c0:c0 + 64],
                                        ckl[:, 0:64], ALU.mult), {},
                                       reads=(bk_, bck), writes=(b1,))
                                R.emit("vector", "tensor_tensor",
                                       (kw2[:, c0:c0 + 64],
                                        PT_[:, c0 + 64:c0 + 128],
                                        skl[:, 0:64], ALU.mult), {},
                                       reads=(bk_, bsk), writes=(b2,))
                                R.emit("vector", "tensor_tensor",
                                       (kt[:, c0:c0 + 64], kw1[:, c0:c0 + 64],
                                        kw2[:, c0:c0 + 64], ALU.subtract), {},
                                       reads=(b1, b2), writes=(bkt,))
                                R.emit("vector", "tensor_tensor",
                                       (kw1[:, c0 + 64:c0 + 128],
                                        PT_[:, c0 + 64:c0 + 128],
                                        ckl[:, 64:128], ALU.mult), {},
                                       reads=(bk_, bck), writes=(b1,))
                                R.emit("vector", "tensor_tensor",
                                       (kw2[:, c0 + 64:c0 + 128],
                                        PT_[:, c0:c0 + 64],
                                        skl[:, 64:128], ALU.mult), {},
                                       reads=(bk_, bsk), writes=(b2,))
                                R.emit("vector", "tensor_tensor",
                                       (kt[:, c0 + 64:c0 + 128],
                                        kw1[:, c0 + 64:c0 + 128],
                                        kw2[:, c0 + 64:c0 + 128], ALU.add), {},
                                       reads=(b1, b2), writes=(bkt,))
                            pb, bb = (Tb, bTb) if pp == 0 else (Tb2, bTb2)
                            for kvl in range(2):
                                R.emit("tensor", "transpose",
                                       (pb[:, kvl * 128:(kvl + 1) * 128],
                                        kt[:, kvl * 128:(kvl + 1) * 128],
                                        ident_s[:]), {},
                                       reads=(bkt, buf("ident_s")), writes=(bb,))
                            R.emit("vector", "tensor_copy",
                                   (KTv[:, :, ti * 128:(ti + 1) * 128],
                                    pb[:, 0:256].rearrange("p (k t) -> p k t",
                                                           k=2)), {},
                                   reads=(bb,), writes=(bKT,))
                            R.emit("vector", "tensor_copy",
                                   (Vv[:, ti, :, 0:128],
                                    PT_[:, 512:768].rearrange(
                                        "p (k j) -> p k j", k=2)), {},
                                   reads=(bv_,), writes=(bV,))
                    R.barrier()
                    esd = contextlib.ExitStack()
                    with esd:
                        pts = [sbt(esd, f"pt{p}{i}", [128, 512], BF16)
                               for i in range(2)]
                        lsc = [sbt(esd, f"lsc{p}{i}", [128, 512], F32)
                               for i in range(2)]
                        mneg = sbt(esd, f"mneg{p}", [128, 2048], F32)
                        avn = [sbt(esd, f"avn{p}{i}", [128, 128], BF16)
                               for i in range(2)]
                        att_t = [sbt(esd, f"att{p}{i}", [128, 128], BF16)
                                 for i in range(2)]
                        rs_ = sbt(esd, f"rs{p}", [128, 8], F32)
                        bmn, brs = buf(mneg.name), buf(rs_.name)
                        avp = [T1[:, 0:512], T1[:, 512:1024], T2[:, 0:512],
                               T2[:, 512:1024]]
                        avb = [bT["1"][0], bT["1"][1], bT["2"][0], bT["2"][1]]
                        for si in range(8):
                            g, a = si // 2, si % 2
                            nkb = 8 * g + 4 + 4 * a
                            win0 = 8 * g + 4 * a
                            for w in range(4):
                                slot = si * 4 + w
                                R.emit("vector", "tensor_scalar",
                                       (mneg[:, w * 512:(w + 1) * 512],
                                        dmat_s[:], theta_s[:, slot:slot + 1],
                                        NEG, ALU.is_gt, ALU.mult), {},
                                       reads=(buf("dmat_s"), buf("theta_s")),
                                       writes=(bmn,))
                            for kvl in range(2):
                                for kb in range(nkb):
                                    pl = kb % 2
                                    lg = T0[:, pl * 512:(pl + 1) * 512]
                                    blg = bT["0"][pl]
                                    kst = KTv[:, kvl, kb * 128:(kb + 1) * 128]
                                    R.emit("tensor", "ldweights", (kst,), {},
                                           reads=(bKT,))
                                    R.emit("tensor", "matmul",
                                           (lg, kst, QTr[:, kvl, si, :]),
                                           {"start": True, "stop": True},
                                           reads=(bKT, bQT), writes=(blg,))
                                    ptb = pts[pl]
                                    bpt = buf(ptb.name)
                                    if kb >= win0:
                                        w = kb - win0
                                        ls = lsc[pl]
                                        bls = buf(ls.name)
                                        R.emit("vector", "tensor_tensor",
                                               (ls[:], lg,
                                                mneg[:, w * 512:(w + 1) * 512],
                                                ALU.add), {},
                                               reads=(blg, bmn), writes=(bls,))
                                        R.emit("scalar", "activation",
                                               (ptb[:], ls[:], AF.Exp),
                                               {"scale": SCALE},
                                               reads=(bls,), writes=(bpt,))
                                    else:
                                        R.emit("scalar", "activation",
                                               (ptb[:], lg, AF.Exp),
                                               {"scale": SCALE},
                                               reads=(blg,), writes=(bpt,))
                                    for q4 in range(4):
                                        pw = ptb[:, q4 * 128:(q4 + 1) * 128]
                                        R.emit("tensor", "ldweights", (pw,),
                                               {}, reads=(bpt,))
                                        R.emit("tensor", "matmul",
                                               (avp[q4][:, 0:129], pw,
                                                Vv[:, kb, kvl, 0:129]),
                                               {"start": kb == 0,
                                                "stop": kb == nkb - 1},
                                               reads=(bpt, bV),
                                               writes=(avb[q4],))
                                for q4 in range(4):
                                    qh = 8 * p + kvl * 4 + q4
                                    R.emit("vector", "reciprocal",
                                           (rs_[:, q4:q4 + 1],
                                            avp[q4][:, 128:129]), {},
                                           reads=(avb[q4],), writes=(brs,))
                                    av_ = avn[q4 % 2]
                                    bav = buf(av_.name)
                                    R.emit("vector", "tensor_scalar",
                                           (av_[:], avp[q4][:, 0:128],
                                            rs_[:, q4:q4 + 1], None, ALU.mult),
                                           {}, reads=(avb[q4], brs),
                                           writes=(bav,))
                                    tbs = Tb[:, (q4 % 2) * 128:
                                             (q4 % 2) * 128 + 128]
                                    R.emit("tensor", "transpose",
                                           (tbs, av_[:], ident_s[:]), {},
                                           reads=(bav, buf("ident_s")),
                                           writes=(bTb,))
                                    at_ = att_t[q4 % 2]
                                    bat = buf(at_.name)
                                    R.emit("vector", "tensor_copy",
                                           (at_[:], tbs), {},
                                           reads=(bTb,), writes=(bat,))
                                    dma("sync", attn_dram[qh, si], at_[:],
                                        (bat,), (battn[qh][si],), "qs")
                        R.barrier()
        esw = contextlib.ExitStack()
        with esw:
            atsb = [sbt(esw, f"atsb{i}", [128, HT * 128], BF16) for i in range(4)]
            wol = [sbt(esw, f"wol{i}", [128, 512], F32) for i in range(2)]
            wob = [sbt(esw, f"wob{i}", [128, 512], BF16) for i in range(2)]
            xsl = [sbt(esw, f"xsl{i}", [128, 512], F32) for i in range(2)]
            pst = [sbt(esw, f"pst{i}", [128, 512], F32) for i in range(2)]
            wops = [T0[:, 0:512], T0[:, 512:1024], T1[:, 0:512], T1[:, 512:1024]]
            wopb = [bT["0"][0], bT["0"][1], bT["1"][0], bT["1"][1]]
            for sig in range(2):
                for s4 in range(4):
                    si = sig * 4 + s4
                    ats = atsb[s4]
                    bats = buf(ats.name)
                    for qh in range(32):
                        dma("sync", ats[:, qh * 128:(qh + 1) * 128],
                            attn_dram[qh, si], (battn[qh][si],), (bats,), "qs")
                for ocq in range(8):
                    for ht in range(HT):
                        wl, wb_ = wol[ht % 2], wob[ht % 2]
                        bwl, bwb = buf(wl.name), buf(wb_.name)
                        dma("gpsimd", wl[:], wo4[ocq, ht], (), (bwl,), "qw")
                        R.emit("gpsimd", "tensor_copy", (wb_[:], wl[:]), {},
                               reads=(bwl,), writes=(bwb,))
                        for s4 in range(4):
                            ats = atsb[s4]
                            lhs = ats[:, ht * 128:(ht + 1) * 128]
                            R.emit("tensor", "ldweights", (lhs,), {},
                                   reads=(buf(ats.name),))
                            R.emit("tensor", "matmul", (wops[s4], lhs, wb_[:]),
                                   {"start": ht == 0, "stop": ht == HT - 1},
                                   reads=(buf(ats.name), bwb),
                                   writes=(wopb[s4],))
                    for s4 in range(4):
                        si = sig * 4 + s4
                        xs = xsl[s4 % 2]
                        bxs = buf(xs.name)
                        dma("sync", xs[:],
                            xq[si * 128:(si + 1) * 128,
                               ocq * 512:(ocq + 1) * 512], (), (bxs,), "qs")
                        ps = pst[s4 % 2]
                        bps = buf(ps.name)
                        R.emit("vector", "tensor_tensor",
                               (ps[:], wops[s4], xs[:], ALU.add), {},
                               reads=(wopb[s4], bxs), writes=(bps,))
                        dma("sync",
                            out_d[si * 128:(si + 1) * 128,
                                  ocq * 512:(ocq + 1) * 512],
                            ps[:], (bps,), (OUT[si][ocq],), "st")
        R.barrier()
        for half in range(2):
            esf = contextlib.ExitStack()
            with esf:
                inter = sbt(esf, f"inter{half}", [128, ISUB * 512], BF16)
                binter = buf(inter.name)
                esf1 = contextlib.ExitStack()
                with esf1:
                    xnh = sbt(esf1, f"xnh{half}", [128, HT * 512], BF16)
                    x_f = sbt(esf1, f"fxl{half}", [128, H], F32)
                    xn_f = sbt(esf1, f"fxn{half}", [128, H], BF16)
                    gsb = [sbt(esf1, f"gsb{half}{i}", [128, 4096], BF16)
                           for i in range(2)]
                    usb = [sbt(esf1, f"usb{half}{i}", [128, 4096], BF16)
                           for i in range(2)]
                    g1s = sbt(esf1, f"g1s{half}", [128, 512], F32)
                    g2s = sbt(esf1, f"g2s{half}", [128, 512], F32)
                    u1s = sbt(esf1, f"u1s{half}", [128, 512], F32)
                    bg1, bg2, bu1 = buf(g1s.name), buf(g2s.name), buf(u1s.name)
                    xnhv = xnh[:].rearrange("p (h t) -> p h t", h=HT)
                    for s4 in range(4):
                        si = half * 4 + s4
                        xn_build(32 + si, xq[si * 128:(si + 1) * 128, :],
                                 xnhv[:, :, s4 * 128:(s4 + 1) * 128],
                                 x_f, xn_f)
                    for isub in range(ISUB):
                        pp = isub % 2
                        gl, ul = gsb[pp], usb[pp]
                        bgl, bul = buf(gl.name), buf(ul.name)
                        dma("gpsimd", gl[:], gate4[isub], (), (bgl,), "qw")
                        dma("gpsimd", ul[:], up4[isub], (), (bul,), "qw")
                        PT_ = T0 if pp == 0 else T1
                        bg_, bu_ = bT[str(pp)]
                        for ht in range(HT):
                            gw = gl[:, ht * 128:(ht + 1) * 128]
                            R.emit("tensor", "ldweights", (gw,), {},
                                   reads=(bgl,))
                            R.emit("tensor", "matmul",
                                   (PT_[:, 0:512], gw, xnhv[:, ht, :]),
                                   {"start": ht == 0, "stop": ht == HT - 1},
                                   reads=(bgl, buf(xnh.name)), writes=(bg_,))
                            uw = ul[:, ht * 128:(ht + 1) * 128]
                            R.emit("tensor", "ldweights", (uw,), {},
                                   reads=(bul,))
                            R.emit("tensor", "matmul",
                                   (PT_[:, 512:1024], uw, xnhv[:, ht, :]),
                                   {"start": ht == 0, "stop": ht == HT - 1},
                                   reads=(bul, buf(xnh.name)), writes=(bu_,))
                        R.emit("vector", "tensor_scalar",
                               (g1s[:], PT_[:, 0:512], gs_s[:, isub:isub + 1],
                                None, ALU.mult), {},
                               reads=(bg_, buf("gs_s")), writes=(bg1,))
                        R.emit("scalar", "activation",
                               (g2s[:], g1s[:], AF.Silu), {},
                               reads=(bg1,), writes=(bg2,))
                        R.emit("vector", "tensor_scalar",
                               (u1s[:], PT_[:, 512:1024],
                                us_s[:, isub:isub + 1], None, ALU.mult), {},
                               reads=(bu_, buf("us_s")), writes=(bu1,))
                        R.emit("vector", "tensor_tensor",
                               (inter[:, isub * 512:(isub + 1) * 512],
                                g2s[:], u1s[:], ALU.mult), {},
                               reads=(bg2, bu1), writes=(binter,))
                R.barrier()
                esf2 = contextlib.ExitStack()
                with esf2:
                    dsb = [sbt(esf2, f"dsb{half}{i}", [128, I], BF16)
                           for i in range(2)]
                    dsc = sbt(esf2, f"dsc{half}", [128, 512], BF16)
                    osb = [sbt(esf2, f"osb{half}{i}", [128, 128], F32)
                           for i in range(2)]
                    ost = [sbt(esf2, f"ost{half}{i}", [128, 128], F32)
                           for i in range(2)]
                    bdsc = buf(dsc.name)
                    for ht in range(HT):
                        pp = ht % 2
                        dl = dsb[pp]
                        bdl = buf(dl.name)
                        dma("gpsimd", dl[:], down4[ht], (), (bdl,), "qw")
                        pd = T2[:, pp * 512:(pp + 1) * 512]
                        bpd = bT["2"][pp]
                        for isub in range(ISUB):
                            dw = dl[:, isub * 128:(isub + 1) * 128]
                            R.emit("tensor", "ldweights", (dw,), {},
                                   reads=(bdl,))
                            R.emit("tensor", "matmul",
                                   (pd, dw,
                                    inter[:, isub * 512:(isub + 1) * 512]),
                                   {"start": isub == 0,
                                    "stop": isub == ISUB - 1},
                                   reads=(bdl, binter), writes=(bpd,))
                        R.emit("vector", "tensor_scalar",
                               (dsc[:], pd, ds_s[:, ht:ht + 1], None,
                                ALU.mult), {},
                               reads=(bpd, buf("ds_s")), writes=(bdsc,))
                        for s4 in range(4):
                            R.emit("tensor", "transpose",
                                   (Tb2[:, s4 * 128:(s4 + 1) * 128],
                                    dsc[:, s4 * 128:(s4 + 1) * 128],
                                    ident_s[:]), {},
                                   reads=(bdsc, buf("ident_s")),
                                   writes=(bTb2,))
                        for s4 in range(4):
                            si = half * 4 + s4
                            ob, ot_ = osb[s4 % 2], ost[s4 % 2]
                            bob, bot = buf(ob.name), buf(ot_.name)
                            dma("sync", ob[:],
                                out_d[si * 128:(si + 1) * 128,
                                      ht * 128:(ht + 1) * 128],
                                (OUT[si][ht // 4],), (bob,), "qs")
                            R.emit("vector", "tensor_tensor",
                                   (ot_[:], Tb2[:, s4 * 128:(s4 + 1) * 128],
                                    ob[:], ALU.add), {},
                                   reads=(bTb2, bob), writes=(bot,))
                            dma("sync",
                                out_d[si * 128:(si + 1) * 128,
                                      ht * 128:(ht + 1) * 128],
                                ot_[:], (bot,), (OUT[si][ht // 4],), "st")
            R.barrier()
        R.raw_wait("sync", "st", R.count["st"])

        with nc.Block() as blk:
            def replay(eng_name, e):
                for waits, method, args, kwargs, sem, inc in R.streams[eng_name]:
                    for s, v in waits:
                        e.wait_ge(semh[s], v)
                    if method is not None:
                        inst = getattr(e, method)(*args, **kwargs)
                        inst.then_inc(semh[sem], inc)

            @blk.sync
            def _(e):
                replay("sync", e)

            @blk.tensor
            def _(e):
                replay("tensor", e)

            @blk.vector
            def _(e):
                replay("vector", e)

            @blk.scalar
            def _(e):
                replay("scalar", e)

            @blk.gpsimd
            def _(e):
                replay("gpsimd", e)

    return nc



def kernel(L0_attn_hidden_states, L0_attn_Wq, L0_attn_Wk, L0_attn_Wv,
           L0_attn_Wo, L0_ffn_gate_wq, L0_ffn_up_wq, L0_ffn_down_wq,
           L0_ffn_gate_scale, L0_ffn_up_scale, L0_ffn_down_scale):
    x = np.ascontiguousarray(np.asarray(L0_attn_hidden_states, dtype=np.float32))
    Wq = np.asarray(L0_attn_Wq, dtype=np.float32)
    Wk = np.asarray(L0_attn_Wk, dtype=np.float32)
    Wv = np.asarray(L0_attn_Wv, dtype=np.float32)
    Wo = np.asarray(L0_attn_Wo, dtype=np.float32)
    gq = np.asarray(L0_ffn_gate_wq)
    uq = np.asarray(L0_ffn_up_wq)
    dq = np.asarray(L0_ffn_down_wq)
    gs = np.asarray(L0_ffn_gate_scale, dtype=np.float32)
    us = np.asarray(L0_ffn_up_scale, dtype=np.float32)
    ds = np.asarray(L0_ffn_down_scale, dtype=np.float32)

    c = np.ascontiguousarray
    wq4 = c(Wq.T.reshape(32, 128, 32, 128).transpose(2, 1, 0, 3)
            .reshape(32, 128, 4096)).astype(np.float32)
    wk4 = c(Wk.T.reshape(32, 128, 4, 256).transpose(2, 1, 0, 3)
            .reshape(4, 128, 32 * 256)).astype(np.float32)
    wv4 = c(Wv.T.reshape(32, 128, 4, 256).transpose(2, 1, 0, 3)
            .reshape(4, 128, 32 * 256)).astype(np.float32)
    wo4 = c(Wo.T.reshape(32, 128, 8, 512).transpose(2, 0, 1, 3)).astype(np.float32)
    gate4 = c(gq.T.astype(BF).reshape(32, 128, ISUB, 128)
              .transpose(2, 1, 0, 3).reshape(ISUB, 128, 4096))
    up4 = c(uq.T.astype(BF).reshape(32, 128, ISUB, 128)
            .transpose(2, 1, 0, 3).reshape(ISUB, 128, 4096))
    down4 = c(dq.T.astype(BF).reshape(ISUB, 128, 32, 128)
              .transpose(2, 1, 0, 3).reshape(32, 128, I))
    gs2 = c(gs.reshape(ISUB, 128).T)
    us2 = c(us.reshape(ISUB, 128).T)
    ds2 = c(ds.reshape(32, 128).T)

    inv = 1.0 / (10000.0 ** (np.arange(0, D, 2) / D))
    t = np.arange(S)
    emb = np.concatenate([np.outer(t, inv)] * 2, axis=1)
    cosT = c(np.cos(emb).T.astype(np.float32))
    sinT = c(np.sin(emb).T.astype(np.float32))
    ck = c(np.cos(emb).astype(np.float32))
    sk = c(np.sin(emb).astype(np.float32))
    dmat4 = c(np.tile((np.arange(128)[:, None] - np.arange(128)[None, :])
                      .astype(np.float32), (1, 4)))
    Rm = np.zeros((128, 128), np.float32)
    for i in range(64):
        Rm[i, i + 64] = -1.0
        Rm[i + 64, i] = 1.0
    rt = c(Rm.T)
    idb = np.eye(128).astype(BF)

    nc = build_nc()
    in_maps = []
    meta = []
    for core in range(8):
        b, k = core // 4, core % 4
        pos = []
        for g in range(4):
            pos += [8 * g + k, 8 * g + 7 - k]
        qrows = np.concatenate([np.arange(p * 128, (p + 1) * 128) for p in pos])
        xqc = c(x[b][qrows])
        cosq = c(cosT[:, qrows])
        sinq = c(sinT[:, qrows])
        th = np.zeros(32, np.float32)
        for si in range(8):
            g, a = si // 2, si % 2
            win0 = 8 * g + 4 * a
            for w in range(4):
                th[si * 4 + w] = (pos[si] - (win0 + w)) * 128.0
        theta = c(np.tile(th[None, :], (128, 1)))
        in_maps.append(dict(
            xp=x[b], xq=xqc, wq4=wq4, wk4=wk4, wv4=wv4, wo4=wo4,
            gate4=gate4, up4=up4, down4=down4, gs=gs2, us=us2, ds=ds2,
            cosq=cosq, sinq=sinq, ck=ck, sk=sk, dmat=dmat4, theta=theta,
            rt=rt, idb=idb))
        meta.append((b, pos))

    res = run_bass_kernel_spmd(nc, in_maps, core_ids=list(range(8)))
    out = np.empty((2, S, H), np.float32)
    for core in range(8):
        b, pos = meta[core]
        o = np.asarray(res.results[core]["out"])
        for si, p in enumerate(pos):
            out[b, p * 128:(p + 1) * 128, :] = o[si * 128:(si + 1) * 128, :]
    return out


if __name__ == "__main__":
    import reference
    inp = reference.setup_inputs()
    inp = {k: np.asarray(v) for k, v in inp.items()}
    got = kernel(**inp)
    ref = np.asarray(reference.reference(**inp))
    err = np.mean((got - ref) ** 2) / np.mean(ref ** 2)
    print("resvar", err, "max", np.abs(got - ref).max())


# version 03 (final)
# speedup vs baseline: 1.3164x; 1.3164x over previous
import contextlib
import math
import numpy as np
import ml_dtypes
import concourse.bass as bass
import concourse.mybir as mybir
from concourse.bass_utils import run_bass_kernel_spmd

F32 = mybir.dt.float32
BF16 = mybir.dt.bfloat16
AF = mybir.ActivationFunctionType
ALU = mybir.AluOpType
BF = ml_dtypes.bfloat16

S = 4096
H = 4096
D = 128
I = 11008
ISUB = I // 128
HT = H // 128
TILES = S // 128
NPASS = 4
EPS = 1e-5
SCALE = 1.0 / math.sqrt(D)
NEG = -1e9

ENGINES = ["sync", "tensor", "vector", "scalar", "gpsimd"]
SEMS = {"pe": 1, "dv": 1, "ac": 1, "po": 1, "qs": 16, "qw": 16, "st": 16}
ENG_SEM = {"tensor": "pe", "vector": "dv", "scalar": "ac", "gpsimd": "po"}


class Buf:
    __slots__ = ("name", "writer", "readers")

    def __init__(self, name):
        self.name = name
        self.writer = None
        self.readers = []


class Rec:
    def __init__(self):
        self.streams = {e: [] for e in ENGINES}
        self.count = {s: 0 for s in SEMS}
        self.waited = {e: {s: 0 for s in SEMS} for e in ENGINES}

    def _w(self, eng, waits, tok):
        if tok is None:
            return
        sem, val = tok
        if eng == "tensor" and sem == "pe":
            return
        if self.waited[eng][sem] >= val:
            return
        self.waited[eng][sem] = val
        waits.append((sem, val))

    def emit(self, eng, method, args=(), kwargs=None, reads=(), writes=(),
             sem=None):
        waits = []
        for b in reads:
            self._w(eng, waits, b.writer)
        for b in writes:
            self._w(eng, waits, b.writer)
            for t in b.readers:
                self._w(eng, waits, t)
        if sem is None:
            sem = ENG_SEM.get(eng, "qs")
        inc = SEMS[sem]
        self.count[sem] += inc
        tok = (sem, self.count[sem])
        self.streams[eng].append((waits, method, args, kwargs or {}, sem, inc))
        for b in reads:
            b.readers.append(tok)
        for b in writes:
            b.writer = tok
            b.readers = []
        return tok

    def raw_wait(self, eng, sem, val):
        if val <= 0 or self.waited[eng][sem] >= val:
            return
        self.waited[eng][sem] = val
        self.streams[eng].append(([(sem, val)], None, (), {}, None, 0))

    def barrier(self):
        snap = dict(self.count)
        for e in ENGINES:
            for s, v in snap.items():
                self.raw_wait(e, s, v)


def build_nc():
    nc = bass.Bass("TRN2", target_bir_lowering=False, debug=False)
    R = Rec()

    di = lambda n, sh, dt: nc.dram_tensor(n, sh, dt, kind="ExternalInput").ap()
    xp = di("xp", [S, H], F32)
    xq = di("xq", [1024, H], F32)
    wq4 = di("wq4", [32, 128, 4096], F32)
    wk4 = di("wk4", [NPASS, 128, HT * 256], F32)
    wv4 = di("wv4", [NPASS, 128, HT * 256], F32)
    wo4 = di("wo4", [8, 32, 128, 512], F32)
    gate4 = di("gate4", [ISUB, 128, 4096], BF16)
    up4 = di("up4", [ISUB, 128, 4096], BF16)
    down4 = di("down4", [32, 128, I], BF16)
    gs_d = di("gs", [128, ISUB], F32)
    us_d = di("us", [128, ISUB], F32)
    ds_d = di("ds", [128, 32], F32)
    cosq_d = di("cosq", [128, 1024], F32)
    sinq_d = di("sinq", [128, 1024], F32)
    ck_d = di("ck", [S, 128], F32)
    sk_d = di("sk", [S, 128], F32)
    dmat_d = di("dmat", [128, 512], F32)
    theta_d = di("theta", [128, 32], F32)
    rt_d = di("rt", [128, 128], F32)
    idb_d = di("idb", [128, 128], BF16)
    out_d = nc.dram_tensor("out", [1024, H], F32, kind="ExternalOutput").ap()
    attn_dram = nc.dram_tensor("attscr", [32, 8, 128, 128], BF16,
                               kind="Internal").ap()

    B = {}

    def buf(n):
        if n not in B:
            B[n] = Buf(n)
        return B[n]

    def dma(eng, dst_ap, src_ap, reads, writes, sem):
        return R.emit(eng, "dma_start", (), {"out": dst_ap, "in_": src_ap},
                      reads=reads, writes=writes, sem=sem)

    es = contextlib.ExitStack()
    with es:
        sbt = lambda st_, n, sh, dt: st_.enter_context(nc.sbuf_tensor(n, sh, dt))
        T0 = es.enter_context(nc.psum_tensor("T0", [128, 1024], F32))
        T1 = es.enter_context(nc.psum_tensor("T1", [128, 1024], F32))
        T2 = es.enter_context(nc.psum_tensor("T2", [128, 1024], F32))
        Tb = es.enter_context(nc.psum_tensor("Tb", [128, 1024], BF16))
        Tb2 = es.enter_context(nc.psum_tensor("Tb2", [128, 1024], BF16))
        semh = {s: es.enter_context(nc.semaphore(s)) for s in SEMS}
        bT = {"0": (buf("T0a"), buf("T0b")), "1": (buf("T1a"), buf("T1b")),
              "2": (buf("T2a"), buf("T2b"))}
        bTb, bTb2 = buf("Tb"), buf("Tb2")
        OUT = [[buf(f"out_{si}_{oc}") for oc in range(8)] for si in range(8)]
        battn = [[buf(f"at_{qh}_{si}") for si in range(8)] for qh in range(32)]

        ident_s = sbt(es, "ident_s", [128, 128], BF16)
        rt_s = sbt(es, "rt_s", [128, 128], F32)
        dmat_s = sbt(es, "dmat_s", [128, 512], F32)
        theta_s = sbt(es, "theta_s", [128, 32], F32)
        gs_s = sbt(es, "gs_s", [128, ISUB], F32)
        us_s = sbt(es, "us_s", [128, ISUB], F32)
        ds_s = sbt(es, "ds_s", [128, 32], F32)
        mr_s = sbt(es, "mr_s", [128, 80], F32)
        ln1 = sbt(es, "ln1", [128, 8], F32)
        for dst, src in [(ident_s, idb_d), (rt_s, rt_d), (dmat_s, dmat_d),
                         (theta_s, theta_d), (gs_s, gs_d), (us_s, us_d),
                         (ds_s, ds_d)]:
            dma("sync", dst[:], src, (), (buf(dst.name),), "qs")

        stats_done = set()

        def xn_build(slot, rows_ap, dst_view, xl, xnl, sq=None):
            bx, bxn = buf(xl.name), buf(xnl.name)
            bmr, bln = buf("mr_s"), buf("ln1")
            dma("sync", xl[:], rows_ap, (), (bx,), "qs")
            rc = mr_s[:, 2 * slot:2 * slot + 1]
            ng = mr_s[:, 2 * slot + 1:2 * slot + 2]
            if slot not in stats_done:
                stats_done.add(slot)
                bsq = buf(sq.name)
                R.emit("scalar", "activation", (sq[:], xl[:], AF.Square),
                       {"accum_out": ln1[:, 0:1]}, reads=(bx,),
                       writes=(bsq, bln))
                R.emit("vector", "tensor_reduce",
                       (ln1[:, 1:2], xl[:], mybir.AxisListType.X, ALU.add),
                       {}, reads=(bx,), writes=(bln,))
                R.emit("vector", "tensor_scalar",
                       (ln1[:, 2:3], ln1[:, 1:2], 1.0 / H, None, ALU.mult),
                       {}, reads=(bln,), writes=(bln,))
                R.emit("vector", "tensor_scalar",
                       (ln1[:, 3:4], ln1[:, 0:1], 1.0 / H, None, ALU.mult),
                       {}, reads=(bln,), writes=(bln,))
                R.emit("vector", "tensor_tensor",
                       (ln1[:, 4:5], ln1[:, 2:3], ln1[:, 2:3], ALU.mult),
                       {}, reads=(bln,), writes=(bln,))
                R.emit("vector", "tensor_tensor",
                       (ln1[:, 5:6], ln1[:, 3:4], ln1[:, 4:5], ALU.subtract),
                       {}, reads=(bln,), writes=(bln,))
                R.emit("vector", "tensor_scalar",
                       (ln1[:, 6:7], ln1[:, 5:6], 1.0, EPS, ALU.mult, ALU.add),
                       {}, reads=(bln,), writes=(bln,))
                R.emit("scalar", "activation",
                       (ln1[:, 7:8], ln1[:, 6:7], AF.Sqrt), {},
                       reads=(bln,), writes=(bln,))
                R.emit("vector", "reciprocal", (rc, ln1[:, 7:8]), {},
                       reads=(bln,), writes=(bmr,))
                R.emit("vector", "tensor_scalar",
                       (ln1[:, 0:1], ln1[:, 2:3], -1.0, None, ALU.mult),
                       {}, reads=(bln,), writes=(bln,))
                R.emit("vector", "tensor_tensor",
                       (ng, ln1[:, 0:1], rc, ALU.mult), {},
                       reads=(bln, bmr), writes=(bmr,))
            else:
                assert slot in stats_done
            R.emit("vector", "tensor_scalar",
                   (xnl[:], xl[:], rc, ng, ALU.mult, ALU.add), {},
                   reads=(bx, bmr), writes=(bxn,))
            bdst = buf(dst_view.tensor.name)
            for grp in range(4):
                pb, bb = (Tb, bTb) if grp % 2 == 0 else (Tb2, bTb2)
                for k in range(8):
                    ht = grp * 8 + k
                    R.emit("tensor", "transpose",
                           (pb[:, k * 128:(k + 1) * 128],
                            xnl[:, ht * 128:(ht + 1) * 128], ident_s[:]),
                           {}, reads=(bxn, buf("ident_s")), writes=(bb,))
                R.emit("vector", "tensor_copy",
                       (dst_view[:, grp * 8:grp * 8 + 8, :],
                        pb[:].rearrange("p (k t) -> p k t", k=8)),
                       {}, reads=(bb,), writes=(bdst,))

        for p in range(NPASS):
            esp = contextlib.ExitStack()
            with esp:
                QT = sbt(esp, f"QT{p}", [128, 8192], BF16)
                bQT = buf(QT.name)
                QTw = QT[:].rearrange("p (k s q t) -> p k s q t", k=2, s=8, q=4)
                QTr = QT[:].rearrange("p (k s t) -> p k s t", k=2, s=8)
                esb = contextlib.ExitStack()
                with esb:
                    cosq_s = sbt(esb, f"cosq{p}", [128, 1024], F32)
                    sinq_s = sbt(esb, f"sinq{p}", [128, 1024], F32)
                    dma("sync", cosq_s[:], cosq_d, (), (buf(cosq_s.name),), "qs")
                    dma("sync", sinq_s[:], sinq_d, (), (buf(sinq_s.name),), "qs")
                    x_l = [sbt(esb, f"bxl{p}{i}", [128, H], F32) for i in range(2)]
                    xn_l = [sbt(esb, f"bxn{p}{i}", [128, H], BF16) for i in range(2)]
                    sq_s = sbt(esb, f"bsq{p}", [128, H], BF16)
                    xnq = sbt(esb, f"xnq{p}", [128, HT * 512], BF16)
                    wql = [sbt(esb, f"wql{p}{i}", [128, 4096], F32) for i in range(2)]
                    wqb = [sbt(esb, f"wqb{p}{i}", [128, 4096], BF16) for i in range(2)]
                    qpre = sbt(esb, f"qpre{p}", [128, 512], F32)
                    qm1 = sbt(esb, f"qm1{p}", [128, 512], F32)
                    qm2 = sbt(esb, f"qm2{p}", [128, 512], F32)
                    bqpre, bqm1, bqm2 = buf(qpre.name), buf(qm1.name), buf(qm2.name)
                    xnqv = xnq[:].rearrange("p (h t) -> p h t", h=HT)
                    for qhalf in range(2):
                        for s4 in range(4):
                            si = qhalf * 4 + s4
                            xn_build(32 + si, xq[si * 128:(si + 1) * 128, :],
                                     xnqv[:, :, s4 * 128:(s4 + 1) * 128],
                                     x_l[s4 % 2], xn_l[s4 % 2], sq_s)
                        for ot in range(8):
                            qh = 8 * p + ot
                            wl, wb = wql[ot % 2], wqb[ot % 2]
                            bwl, bwb = buf(wl.name), buf(wb.name)
                            dma("gpsimd", wl[:], wq4[qh], (), (bwl,), "qw")
                            R.emit("scalar", "copy", (wb[:], wl[:]), {},
                                   reads=(bwl,), writes=(bwb,))
                            psq = T0[:, 0:512] if ot % 2 == 0 else T0[:, 512:1024]
                            bpsq = bT["0"][ot % 2]
                            for ht in range(HT):
                                lhs = wb[:, ht * 128:(ht + 1) * 128]
                                R.emit("tensor", "ldweights", (lhs,), {},
                                       reads=(bwb,))
                                R.emit("tensor", "matmul",
                                       (psq, lhs, xnqv[:, ht, :]),
                                       {"start": ht == 0, "stop": ht == HT - 1},
                                       reads=(bwb, buf(xnq.name)), writes=(bpsq,))
                            R.emit("vector", "tensor_copy", (qpre[:], psq), {},
                                   reads=(bpsq,), writes=(bqpre,))
                            rot = T1[:, 0:512] if ot % 2 == 0 else T1[:, 512:1024]
                            brot = bT["1"][ot % 2]
                            R.emit("tensor", "matmul", (rot, rt_s[:], qpre[:]),
                                   {"start": True, "stop": True},
                                   reads=(buf("rt_s"), bqpre), writes=(brot,))
                            cq = cosq_s[:, qhalf * 512:(qhalf + 1) * 512]
                            sq2 = sinq_s[:, qhalf * 512:(qhalf + 1) * 512]
                            R.emit("vector", "tensor_tensor",
                                   (qm1[:], qpre[:], cq, ALU.mult), {},
                                   reads=(bqpre, buf(cosq_s.name)), writes=(bqm1,))
                            R.emit("vector", "tensor_tensor",
                                   (qm2[:], rot, sq2, ALU.mult), {},
                                   reads=(brot, buf(sinq_s.name)), writes=(bqm2,))
                            kvl, q4 = ot // 4, ot % 4
                            R.emit("vector", "tensor_tensor",
                                   (QTw[:, kvl, qhalf * 4:qhalf * 4 + 4, q4, :],
                                    qm1[:].rearrange("p (s t) -> p s t", s=4),
                                    qm2[:].rearrange("p (s t) -> p s t", s=4),
                                    ALU.add),
                                   {}, reads=(bqm1, bqm2), writes=(bQT,))
                R.barrier()
                esc = contextlib.ExitStack()
                with esc:
                    KT = sbt(esc, f"KT{p}", [128, 8192], BF16)
                    Vt = sbt(esc, f"V{p}", [128, 32 * 260], BF16)
                    bKT, bV = buf(KT.name), buf(Vt.name)
                    KTv = KT[:].rearrange("p (k t) -> p k t", k=2)
                    Vv = Vt[:].rearrange("p (t k j) -> p t k j", k=2, j=130)
                    esc2 = contextlib.ExitStack()
                    with esc2:
                        x_l = [sbt(esc2, f"cxl{p}{i}", [128, H], F32)
                               for i in range(2)]
                        xn_l = [sbt(esc2, f"cxn{p}{i}", [128, H], BF16)
                                for i in range(2)]
                        sq_s = sbt(esc2, f"csq{p}", [128, H], BF16)
                        wkl = [sbt(esc2, f"wkl{p}{i}", [128, 2048], F32)
                               for i in range(2)]
                        wkb = sbt(esc2, f"wkb{p}", [128, HT * 256], BF16)
                        wvb = sbt(esc2, f"wvb{p}", [128, HT * 256], BF16)
                        xnt = [sbt(esc2, f"xnt{p}{i}", [128, HT * 128], BF16)
                               for i in range(2)]
                        ktmp = [sbt(esc2, f"ktmp{p}{i}", [128, 256], BF16)
                                for i in range(2)]
                        kw1 = sbt(esc2, f"kw1{p}", [128, 256], F32)
                        kw2 = sbt(esc2, f"kw2{p}", [128, 256], F32)
                        ck_l = [sbt(esc2, f"ckl{p}{i}", [128, 128], F32)
                                for i in range(2)]
                        sk_l = [sbt(esc2, f"skl{p}{i}", [128, 128], F32)
                                for i in range(2)]
                        bwkb, bwvb = buf(wkb.name), buf(wvb.name)
                        for wd, wbf, bb_ in [(wk4, wkb, bwkb), (wv4, wvb, bwvb)]:
                            for ch in range(4):
                                wl = wkl[ch % 2]
                                bwl = buf(wl.name)
                                dma("gpsimd", wl[:],
                                    wd[p][:, ch * 2048:(ch + 1) * 2048],
                                    (), (bwl,), "qw")
                                R.emit("scalar", "copy",
                                       (wbf[:, ch * 2048:(ch + 1) * 2048], wl[:]),
                                       {}, reads=(bwl,), writes=(bb_,))
                        R.emit("vector", "memset", (Vt[:], 1.0), {}, reads=(),
                               writes=(bV,))
                        for ti in range(TILES):
                            pp = ti % 2
                            xnow = xnt[pp]
                            bxn = buf(xnow.name)
                            xn_build(ti, xp[ti * 128:(ti + 1) * 128, :],
                                     xnow[:].rearrange("p (h t) -> p h t", h=HT),
                                     x_l[pp], xn_l[pp], sq_s)
                            PT_ = T0 if pp == 0 else T1
                            bk_, bv_ = bT[str(pp)]
                            for ht in range(HT):
                                lhs = xnow[:, ht * 128:(ht + 1) * 128]
                                R.emit("tensor", "ldweights", (lhs,), {},
                                       reads=(bxn,))
                                R.emit("tensor", "matmul",
                                       (PT_[:, 0:256], lhs,
                                        wkb[:, ht * 256:(ht + 1) * 256]),
                                       {"start": ht == 0, "stop": ht == HT - 1},
                                       reads=(bxn, bwkb), writes=(bk_,))
                                R.emit("tensor", "matmul",
                                       (PT_[:, 512:768], lhs,
                                        wvb[:, ht * 256:(ht + 1) * 256]),
                                       {"start": ht == 0, "stop": ht == HT - 1},
                                       reads=(bxn, bwvb), writes=(bv_,))
                            ckl, skl = ck_l[pp], sk_l[pp]
                            bck, bsk = buf(ckl.name), buf(skl.name)
                            dma("sync", ckl[:], ck_d[ti * 128:(ti + 1) * 128, :],
                                (), (bck,), "qs")
                            dma("sync", skl[:], sk_d[ti * 128:(ti + 1) * 128, :],
                                (), (bsk,), "qs")
                            kt = ktmp[pp]
                            bkt = buf(kt.name)
                            b1, b2 = buf(kw1.name), buf(kw2.name)
                            for kvl in range(2):
                                c0 = kvl * 128
                                R.emit("vector", "tensor_tensor",
                                       (kw1[:, c0:c0 + 64], PT_[:, c0:c0 + 64],
                                        ckl[:, 0:64], ALU.mult), {},
                                       reads=(bk_, bck), writes=(b1,))
                                R.emit("vector", "tensor_tensor",
                                       (kw2[:, c0:c0 + 64],
                                        PT_[:, c0 + 64:c0 + 128],
                                        skl[:, 0:64], ALU.mult), {},
                                       reads=(bk_, bsk), writes=(b2,))
                                R.emit("vector", "tensor_tensor",
                                       (kt[:, c0:c0 + 64], kw1[:, c0:c0 + 64],
                                        kw2[:, c0:c0 + 64], ALU.subtract), {},
                                       reads=(b1, b2), writes=(bkt,))
                                R.emit("vector", "tensor_tensor",
                                       (kw1[:, c0 + 64:c0 + 128],
                                        PT_[:, c0 + 64:c0 + 128],
                                        ckl[:, 64:128], ALU.mult), {},
                                       reads=(bk_, bck), writes=(b1,))
                                R.emit("vector", "tensor_tensor",
                                       (kw2[:, c0 + 64:c0 + 128],
                                        PT_[:, c0:c0 + 64],
                                        skl[:, 64:128], ALU.mult), {},
                                       reads=(bk_, bsk), writes=(b2,))
                                R.emit("vector", "tensor_tensor",
                                       (kt[:, c0 + 64:c0 + 128],
                                        kw1[:, c0 + 64:c0 + 128],
                                        kw2[:, c0 + 64:c0 + 128], ALU.add), {},
                                       reads=(b1, b2), writes=(bkt,))
                            pb, bb = (Tb, bTb) if pp == 0 else (Tb2, bTb2)
                            for kvl in range(2):
                                R.emit("tensor", "transpose",
                                       (pb[:, kvl * 128:(kvl + 1) * 128],
                                        kt[:, kvl * 128:(kvl + 1) * 128],
                                        ident_s[:]), {},
                                       reads=(bkt, buf("ident_s")), writes=(bb,))
                            R.emit("vector", "tensor_copy",
                                   (KTv[:, :, ti * 128:(ti + 1) * 128],
                                    pb[:, 0:256].rearrange("p (k t) -> p k t",
                                                           k=2)), {},
                                   reads=(bb,), writes=(bKT,))
                            R.emit("vector", "tensor_copy",
                                   (Vv[:, ti, :, 0:128],
                                    PT_[:, 512:768].rearrange(
                                        "p (k j) -> p k j", k=2)), {},
                                   reads=(bv_,), writes=(bV,))
                    R.barrier()
                    esd = contextlib.ExitStack()
                    with esd:
                        pta = sbt(esd, f"pt{p}", [128, 32 * 512], BF16)
                        bpta = buf(pta.name)
                        lsc = [sbt(esd, f"lsc{p}{i}", [128, 512], F32)
                               for i in range(2)]
                        mneg = sbt(esd, f"mneg{p}", [128, 2048], F32)
                        avn = [sbt(esd, f"avn{p}{i}", [128, 128], BF16)
                               for i in range(2)]
                        att_t = [sbt(esd, f"att{p}{i}", [128, 128], BF16)
                                 for i in range(2)]
                        rs_ = sbt(esd, f"rs{p}", [128, 8], F32)
                        bmn, brs = buf(mneg.name), buf(rs_.name)
                        avp = [T1[:, 0:512], T1[:, 512:1024], T2[:, 0:512],
                               T2[:, 512:1024]]
                        avb = [bT["1"][0], bT["1"][1], bT["2"][0], bT["2"][1]]
                        for si in range(8):
                            g, a = si // 2, si % 2
                            nkb = 8 * g + 4 + 4 * a
                            win0 = 8 * g + 4 * a
                            for w in range(4):
                                slot = si * 4 + w
                                R.emit("vector", "tensor_scalar",
                                       (mneg[:, w * 512:(w + 1) * 512],
                                        dmat_s[:], theta_s[:, slot:slot + 1],
                                        NEG, ALU.is_gt, ALU.mult), {},
                                       reads=(buf("dmat_s"), buf("theta_s")),
                                       writes=(bmn,))
                            for kvl in range(2):
                                for kb in range(nkb):
                                    pl = kb % 2
                                    lg = T0[:, pl * 512:(pl + 1) * 512]
                                    blg = bT["0"][pl]
                                    kst = KTv[:, kvl, kb * 128:(kb + 1) * 128]
                                    R.emit("tensor", "ldweights", (kst,), {},
                                           reads=(bKT,))
                                    R.emit("tensor", "matmul",
                                           (lg, kst, QTr[:, kvl, si, :]),
                                           {"start": True, "stop": True},
                                           reads=(bKT, bQT), writes=(blg,))
                                    ptw = pta[:, kb * 512:(kb + 1) * 512]
                                    if kb >= win0:
                                        w = kb - win0
                                        ls = lsc[pl]
                                        bls = buf(ls.name)
                                        R.emit("vector", "tensor_tensor",
                                               (ls[:], lg,
                                                mneg[:, w * 512:(w + 1) * 512],
                                                ALU.add), {},
                                               reads=(blg, bmn), writes=(bls,))
                                        R.emit("scalar", "activation",
                                               (ptw, ls[:], AF.Exp),
                                               {"scale": SCALE},
                                               reads=(bls,), writes=(bpta,))
                                    else:
                                        R.emit("scalar", "activation",
                                               (ptw, lg, AF.Exp),
                                               {"scale": SCALE},
                                               reads=(blg,), writes=(bpta,))
                                for kb in range(nkb):
                                    for q4 in range(4):
                                        pw = pta[:, kb * 512 + q4 * 128:
                                                 kb * 512 + (q4 + 1) * 128]
                                        R.emit("tensor", "ldweights", (pw,),
                                               {}, reads=(bpta,))
                                        R.emit("tensor", "matmul",
                                               (avp[q4][:, 0:129], pw,
                                                Vv[:, kb, kvl, 0:129]),
                                               {"start": kb == 0,
                                                "stop": kb == nkb - 1},
                                               reads=(bpta, bV),
                                               writes=(avb[q4],))
                                for q4 in range(4):
                                    qh = 8 * p + kvl * 4 + q4
                                    R.emit("vector", "reciprocal",
                                           (rs_[:, q4:q4 + 1],
                                            avp[q4][:, 128:129]), {},
                                           reads=(avb[q4],), writes=(brs,))
                                    av_ = avn[q4 % 2]
                                    bav = buf(av_.name)
                                    R.emit("vector", "tensor_scalar",
                                           (av_[:], avp[q4][:, 0:128],
                                            rs_[:, q4:q4 + 1], None, ALU.mult),
                                           {}, reads=(avb[q4], brs),
                                           writes=(bav,))
                                    tbs = Tb[:, (q4 % 2) * 128:
                                             (q4 % 2) * 128 + 128]
                                    R.emit("tensor", "transpose",
                                           (tbs, av_[:], ident_s[:]), {},
                                           reads=(bav, buf("ident_s")),
                                           writes=(bTb,))
                                    at_ = att_t[q4 % 2]
                                    bat = buf(at_.name)
                                    R.emit("vector", "tensor_copy",
                                           (at_[:], tbs), {},
                                           reads=(bTb,), writes=(bat,))
                                    dma("sync", attn_dram[qh, si], at_[:],
                                        (bat,), (battn[qh][si],), "qs")
                        R.barrier()
        esw = contextlib.ExitStack()
        with esw:
            atsb = [sbt(esw, f"atsb{i}", [128, HT * 128], BF16) for i in range(4)]
            wol = [sbt(esw, f"wol{i}", [128, 512], F32) for i in range(2)]
            wob = [sbt(esw, f"wob{i}", [128, 512], BF16) for i in range(2)]
            xsl = [sbt(esw, f"xsl{i}", [128, 512], F32) for i in range(2)]
            pst = [sbt(esw, f"pst{i}", [128, 512], F32) for i in range(2)]
            wops = [T0[:, 0:512], T0[:, 512:1024], T1[:, 0:512], T1[:, 512:1024]]
            wopb = [bT["0"][0], bT["0"][1], bT["1"][0], bT["1"][1]]
            for sig in range(2):
                for s4 in range(4):
                    si = sig * 4 + s4
                    ats = atsb[s4]
                    bats = buf(ats.name)
                    for qh in range(32):
                        dma("sync", ats[:, qh * 128:(qh + 1) * 128],
                            attn_dram[qh, si], (battn[qh][si],), (bats,), "qs")
                for ocq in range(8):
                    for ht in range(HT):
                        wl, wb_ = wol[ht % 2], wob[ht % 2]
                        bwl, bwb = buf(wl.name), buf(wb_.name)
                        dma("gpsimd", wl[:], wo4[ocq, ht], (), (bwl,), "qw")
                        R.emit("scalar", "copy", (wb_[:], wl[:]), {},
                               reads=(bwl,), writes=(bwb,))
                        for s4 in range(4):
                            ats = atsb[s4]
                            lhs = ats[:, ht * 128:(ht + 1) * 128]
                            R.emit("tensor", "ldweights", (lhs,), {},
                                   reads=(buf(ats.name),))
                            R.emit("tensor", "matmul", (wops[s4], lhs, wb_[:]),
                                   {"start": ht == 0, "stop": ht == HT - 1},
                                   reads=(buf(ats.name), bwb),
                                   writes=(wopb[s4],))
                    for s4 in range(4):
                        si = sig * 4 + s4
                        xs = xsl[s4 % 2]
                        bxs = buf(xs.name)
                        dma("sync", xs[:],
                            xq[si * 128:(si + 1) * 128,
                               ocq * 512:(ocq + 1) * 512], (), (bxs,), "qs")
                        ps = pst[s4 % 2]
                        bps = buf(ps.name)
                        R.emit("vector", "tensor_tensor",
                               (ps[:], wops[s4], xs[:], ALU.add), {},
                               reads=(wopb[s4], bxs), writes=(bps,))
                        dma("sync",
                            out_d[si * 128:(si + 1) * 128,
                                  ocq * 512:(ocq + 1) * 512],
                            ps[:], (bps,), (OUT[si][ocq],), "st")
        R.barrier()
        for half in range(2):
            esf = contextlib.ExitStack()
            with esf:
                inter = sbt(esf, f"inter{half}", [128, ISUB * 512], BF16)
                binter = buf(inter.name)
                esf1 = contextlib.ExitStack()
                with esf1:
                    xnh = sbt(esf1, f"xnh{half}", [128, HT * 512], BF16)
                    x_f = sbt(esf1, f"fxl{half}", [128, H], F32)
                    xn_f = sbt(esf1, f"fxn{half}", [128, H], BF16)
                    gsb = [sbt(esf1, f"gsb{half}{i}", [128, 4096], BF16)
                           for i in range(2)]
                    usb = [sbt(esf1, f"usb{half}{i}", [128, 4096], BF16)
                           for i in range(2)]
                    g1s = sbt(esf1, f"g1s{half}", [128, 512], F32)
                    g2s = sbt(esf1, f"g2s{half}", [128, 512], F32)
                    u1s = sbt(esf1, f"u1s{half}", [128, 512], F32)
                    bg1, bg2, bu1 = buf(g1s.name), buf(g2s.name), buf(u1s.name)
                    xnhv = xnh[:].rearrange("p (h t) -> p h t", h=HT)
                    for s4 in range(4):
                        si = half * 4 + s4
                        xn_build(32 + si, xq[si * 128:(si + 1) * 128, :],
                                 xnhv[:, :, s4 * 128:(s4 + 1) * 128],
                                 x_f, xn_f)
                    for isub in range(ISUB):
                        pp = isub % 2
                        gl, ul = gsb[pp], usb[pp]
                        bgl, bul = buf(gl.name), buf(ul.name)
                        dma("gpsimd", gl[:], gate4[isub], (), (bgl,), "qw")
                        dma("gpsimd", ul[:], up4[isub], (), (bul,), "qw")
                        PT_ = T0 if pp == 0 else T1
                        bg_, bu_ = bT[str(pp)]
                        for ht in range(HT):
                            gw = gl[:, ht * 128:(ht + 1) * 128]
                            R.emit("tensor", "ldweights", (gw,), {},
                                   reads=(bgl,))
                            R.emit("tensor", "matmul",
                                   (PT_[:, 0:512], gw, xnhv[:, ht, :]),
                                   {"start": ht == 0, "stop": ht == HT - 1},
                                   reads=(bgl, buf(xnh.name)), writes=(bg_,))
                            uw = ul[:, ht * 128:(ht + 1) * 128]
                            R.emit("tensor", "ldweights", (uw,), {},
                                   reads=(bul,))
                            R.emit("tensor", "matmul",
                                   (PT_[:, 512:1024], uw, xnhv[:, ht, :]),
                                   {"start": ht == 0, "stop": ht == HT - 1},
                                   reads=(bul, buf(xnh.name)), writes=(bu_,))
                        R.emit("vector", "tensor_scalar",
                               (g1s[:], PT_[:, 0:512], gs_s[:, isub:isub + 1],
                                None, ALU.mult), {},
                               reads=(bg_, buf("gs_s")), writes=(bg1,))
                        R.emit("scalar", "activation",
                               (g2s[:], g1s[:], AF.Silu), {},
                               reads=(bg1,), writes=(bg2,))
                        R.emit("vector", "tensor_scalar",
                               (u1s[:], PT_[:, 512:1024],
                                us_s[:, isub:isub + 1], None, ALU.mult), {},
                               reads=(bu_, buf("us_s")), writes=(bu1,))
                        R.emit("vector", "tensor_tensor",
                               (inter[:, isub * 512:(isub + 1) * 512],
                                g2s[:], u1s[:], ALU.mult), {},
                               reads=(bg2, bu1), writes=(binter,))
                R.barrier()
                esf2 = contextlib.ExitStack()
                with esf2:
                    dsb = [sbt(esf2, f"dsb{half}{i}", [128, I], BF16)
                           for i in range(2)]
                    dsc = sbt(esf2, f"dsc{half}", [128, 512], BF16)
                    osb = [sbt(esf2, f"osb{half}{i}", [128, 128], F32)
                           for i in range(2)]
                    ost = [sbt(esf2, f"ost{half}{i}", [128, 128], F32)
                           for i in range(2)]
                    bdsc = buf(dsc.name)
                    for ht in range(HT):
                        pp = ht % 2
                        dl = dsb[pp]
                        bdl = buf(dl.name)
                        dma("gpsimd", dl[:], down4[ht], (), (bdl,), "qw")
                        pd = T2[:, pp * 512:(pp + 1) * 512]
                        bpd = bT["2"][pp]
                        for isub in range(ISUB):
                            dw = dl[:, isub * 128:(isub + 1) * 128]
                            R.emit("tensor", "ldweights", (dw,), {},
                                   reads=(bdl,))
                            R.emit("tensor", "matmul",
                                   (pd, dw,
                                    inter[:, isub * 512:(isub + 1) * 512]),
                                   {"start": isub == 0,
                                    "stop": isub == ISUB - 1},
                                   reads=(bdl, binter), writes=(bpd,))
                        R.emit("vector", "tensor_scalar",
                               (dsc[:], pd, ds_s[:, ht:ht + 1], None,
                                ALU.mult), {},
                               reads=(bpd, buf("ds_s")), writes=(bdsc,))
                        for s4 in range(4):
                            R.emit("tensor", "transpose",
                                   (Tb2[:, s4 * 128:(s4 + 1) * 128],
                                    dsc[:, s4 * 128:(s4 + 1) * 128],
                                    ident_s[:]), {},
                                   reads=(bdsc, buf("ident_s")),
                                   writes=(bTb2,))
                        for s4 in range(4):
                            si = half * 4 + s4
                            ob, ot_ = osb[s4 % 2], ost[s4 % 2]
                            bob, bot = buf(ob.name), buf(ot_.name)
                            dma("sync", ob[:],
                                out_d[si * 128:(si + 1) * 128,
                                      ht * 128:(ht + 1) * 128],
                                (OUT[si][ht // 4],), (bob,), "qs")
                            R.emit("vector", "tensor_tensor",
                                   (ot_[:], Tb2[:, s4 * 128:(s4 + 1) * 128],
                                    ob[:], ALU.add), {},
                                   reads=(bTb2, bob), writes=(bot,))
                            dma("sync",
                                out_d[si * 128:(si + 1) * 128,
                                      ht * 128:(ht + 1) * 128],
                                ot_[:], (bot,), (OUT[si][ht // 4],), "st")
            R.barrier()
        R.raw_wait("sync", "st", R.count["st"])

        with nc.Block() as blk:
            def replay(eng_name, e):
                for waits, method, args, kwargs, sem, inc in R.streams[eng_name]:
                    for s, v in waits:
                        e.wait_ge(semh[s], v)
                    if method is not None:
                        inst = getattr(e, method)(*args, **kwargs)
                        inst.then_inc(semh[sem], inc)

            @blk.sync
            def _(e):
                replay("sync", e)

            @blk.tensor
            def _(e):
                replay("tensor", e)

            @blk.vector
            def _(e):
                replay("vector", e)

            @blk.scalar
            def _(e):
                replay("scalar", e)

            @blk.gpsimd
            def _(e):
                replay("gpsimd", e)

    return nc



def kernel(L0_attn_hidden_states, L0_attn_Wq, L0_attn_Wk, L0_attn_Wv,
           L0_attn_Wo, L0_ffn_gate_wq, L0_ffn_up_wq, L0_ffn_down_wq,
           L0_ffn_gate_scale, L0_ffn_up_scale, L0_ffn_down_scale):
    x = np.ascontiguousarray(np.asarray(L0_attn_hidden_states, dtype=np.float32))
    Wq = np.asarray(L0_attn_Wq, dtype=np.float32)
    Wk = np.asarray(L0_attn_Wk, dtype=np.float32)
    Wv = np.asarray(L0_attn_Wv, dtype=np.float32)
    Wo = np.asarray(L0_attn_Wo, dtype=np.float32)
    gq = np.asarray(L0_ffn_gate_wq)
    uq = np.asarray(L0_ffn_up_wq)
    dq = np.asarray(L0_ffn_down_wq)
    gs = np.asarray(L0_ffn_gate_scale, dtype=np.float32)
    us = np.asarray(L0_ffn_up_scale, dtype=np.float32)
    ds = np.asarray(L0_ffn_down_scale, dtype=np.float32)

    c = np.ascontiguousarray
    wq4 = c(Wq.T.reshape(32, 128, 32, 128).transpose(2, 1, 0, 3)
            .reshape(32, 128, 4096)).astype(np.float32)
    wk4 = c(Wk.T.reshape(32, 128, 4, 256).transpose(2, 1, 0, 3)
            .reshape(4, 128, 32 * 256)).astype(np.float32)
    wv4 = c(Wv.T.reshape(32, 128, 4, 256).transpose(2, 1, 0, 3)
            .reshape(4, 128, 32 * 256)).astype(np.float32)
    wo4 = c(Wo.T.reshape(32, 128, 8, 512).transpose(2, 0, 1, 3)).astype(np.float32)
    gate4 = c(gq.T.astype(BF).reshape(32, 128, ISUB, 128)
              .transpose(2, 1, 0, 3).reshape(ISUB, 128, 4096))
    up4 = c(uq.T.astype(BF).reshape(32, 128, ISUB, 128)
            .transpose(2, 1, 0, 3).reshape(ISUB, 128, 4096))
    down4 = c(dq.T.astype(BF).reshape(ISUB, 128, 32, 128)
              .transpose(2, 1, 0, 3).reshape(32, 128, I))
    gs2 = c(gs.reshape(ISUB, 128).T)
    us2 = c(us.reshape(ISUB, 128).T)
    ds2 = c(ds.reshape(32, 128).T)

    inv = 1.0 / (10000.0 ** (np.arange(0, D, 2) / D))
    t = np.arange(S)
    emb = np.concatenate([np.outer(t, inv)] * 2, axis=1)
    cosT = c(np.cos(emb).T.astype(np.float32))
    sinT = c(np.sin(emb).T.astype(np.float32))
    ck = c(np.cos(emb).astype(np.float32))
    sk = c(np.sin(emb).astype(np.float32))
    dmat4 = c(np.tile((np.arange(128)[:, None] - np.arange(128)[None, :])
                      .astype(np.float32), (1, 4)))
    Rm = np.zeros((128, 128), np.float32)
    for i in range(64):
        Rm[i, i + 64] = -1.0
        Rm[i + 64, i] = 1.0
    rt = c(Rm.T)
    idb = np.eye(128).astype(BF)

    nc = build_nc()
    in_maps = []
    meta = []
    for core in range(8):
        b, k = core // 4, core % 4
        pos = []
        for g in range(4):
            pos += [8 * g + k, 8 * g + 7 - k]
        qrows = np.concatenate([np.arange(p * 128, (p + 1) * 128) for p in pos])
        xqc = c(x[b][qrows])
        cosq = c(cosT[:, qrows])
        sinq = c(sinT[:, qrows])
        th = np.zeros(32, np.float32)
        for si in range(8):
            g, a = si // 2, si % 2
            win0 = 8 * g + 4 * a
            for w in range(4):
                th[si * 4 + w] = (pos[si] - (win0 + w)) * 128.0
        theta = c(np.tile(th[None, :], (128, 1)))
        in_maps.append(dict(
            xp=x[b], xq=xqc, wq4=wq4, wk4=wk4, wv4=wv4, wo4=wo4,
            gate4=gate4, up4=up4, down4=down4, gs=gs2, us=us2, ds=ds2,
            cosq=cosq, sinq=sinq, ck=ck, sk=sk, dmat=dmat4, theta=theta,
            rt=rt, idb=idb))
        meta.append((b, pos))

    res = run_bass_kernel_spmd(nc, in_maps, core_ids=list(range(8)))
    out = np.empty((2, S, H), np.float32)
    for core in range(8):
        b, pos = meta[core]
        o = np.asarray(res.results[core]["out"])
        for si, p in enumerate(pos):
            out[b, p * 128:(p + 1) * 128, :] = o[si * 128:(si + 1) * 128, :]
    return out


if __name__ == "__main__":
    import reference
    inp = reference.setup_inputs()
    inp = {k: np.asarray(v) for k, v in inp.items()}
    got = kernel(**inp)
    ref = np.asarray(reference.reference(**inp))
    err = np.mean((got - ref) ** 2) / np.mean(ref ** 2)
    print("resvar", err, "max", np.abs(got - ref).max())
